# version 01
# baseline (speedup 1.0000x reference)
import numpy as np
import ml_dtypes
from contextlib import ExitStack

import concourse.bass as bass
import concourse.mybir as mybir
from concourse.bass_utils import run_bass_kernel_spmd

BF = ml_dtypes.bfloat16
F32 = mybir.dt.float32
BF16 = mybir.dt.bfloat16
I32 = mybir.dt.int32
AF = mybir.ActivationFunctionType
ALU = mybir.AluOpType

NCORES = 8
SLOT_W = (32, 24, 16, 8)
TOPK = 410
BIS_ITERS = 14
SENT = -1.0e30
BIS_LO, BIS_HI = -384.0, 384.0
SCALE = float(1.0 / np.sqrt(128.0))
D = 4096
S = 4096
B = 2
REV = [0, 4, 1, 5, 2, 6, 3, 7]
I_FFN = 11008
ICH = 86
_NOC = [8]


def blocks_of_core(c):
    return [31 - c, 16 + c, 15 - c, c]


def rows_of_core(c):
    rows = []
    for b in range(B):
        for j in blocks_of_core(c):
            rows.extend(range(b * S + j * 128, b * S + j * 128 + 128))
    return np.array(rows, dtype=np.int64)



def build_weights():
    nc = bass.Bass("TRN2", target_bir_lowering=False, detect_race_conditions=False)
    casts = {
        "wqt": (512, 4096), "wkt": (512, 1024), "wvt": (512, 1024),
        "wot": (512, 4096), "l1wt": (512, 4096),
        "l1at": (512, 64), "l1bt": (8, 4096), "l1gwt": (512, 4),
    }
    ins, outs = {}, {}
    for name, (r, cdim) in casts.items():
        ins[name] = nc.dram_tensor(name, [r, cdim], F32, kind="ExternalInput").ap()
        outs[name] = nc.dram_tensor(name + "_bf", [r, cdim], BF16, kind="ExternalOutput").ap()
    gq = nc.dram_tensor("gq", [4096, 172], I32, kind="ExternalInput").ap()
    gz = nc.dram_tensor("gz", [4096, 172], I32, kind="ExternalInput").ap()
    gs = nc.dram_tensor("gs", [4096, 1376], F32, kind="ExternalInput").ap()
    uq = nc.dram_tensor("uq", [4096, 172], I32, kind="ExternalInput").ap()
    uz = nc.dram_tensor("uz", [4096, 172], I32, kind="ExternalInput").ap()
    us = nc.dram_tensor("us", [4096, 1376], F32, kind="ExternalInput").ap()
    dq = nc.dram_tensor("dq", [1376, 512], I32, kind="ExternalInput").ap()
    dz = nc.dram_tensor("dz", [1376, 512], I32, kind="ExternalInput").ap()
    ds = nc.dram_tensor("ds", [1376, 4096], F32, kind="ExternalInput").ap()
    wg = nc.dram_tensor("wg_bf", [4096, 1376], BF16, kind="ExternalOutput").ap()
    wu = nc.dram_tensor("wu_bf", [4096, 1376], BF16, kind="ExternalOutput").ap()
    wd = nc.dram_tensor("wd_bf", [1376, 4096], BF16, kind="ExternalOutput").ap()

    cast_jobs = []
    for name, (r, cdim) in casts.items():
        for i in range(0, r, 128):
            rr = min(128, r - i)
            cast_jobs.append((ins[name][i:i + rr, :], outs[name][i:i + rr, :], rr, cdim))
    deq_jobs = [(gq, gz, gs, wg, 32, 172), (uq, uz, us, wu, 32, 172),
                (dq, dz, ds, wd, 11, 512)]

    with ExitStack() as es:
        E = es.enter_context
        cst_f = E(nc.sbuf_tensor("cst_f", [128, 2, 4096], F32))
        cst_b = E(nc.sbuf_tensor("cst_b", [128, 2, 4096], BF16))
        qw_sb = E(nc.sbuf_tensor("qw_sb", [128, 2, 512], I32))
        zw_sb = E(nc.sbuf_tensor("zw_sb", [128, 2, 512], I32))
        fs = E(nc.sbuf_tensor("fs", [128, 2, 4096], F32))
        niw = E(nc.sbuf_tensor("niw", [128, 4096], I32))
        niz = E(nc.sbuf_tensor("niz", [128, 4096], I32))
        fw = E(nc.sbuf_tensor("fw", [128, 4096], F32))
        fz = E(nc.sbuf_tensor("fz", [128, 4096], F32))
        ob = E(nc.sbuf_tensor("ob", [128, 2, 4096], BF16))
        ld = E(nc.semaphore("ld"))
        cmp = E(nc.semaphore("cmp"))
        st = E(nc.semaphore("st"))
        block = E(nc.Block())

        NC_JOBS = len(cast_jobs)
        ND_TILES = sum(j[4] for j in deq_jobs)

        @block.sync
        def _(sync):
            nld = 0
            nst = 0
            for i, (src, dst, rr, cdim) in enumerate(cast_jobs):
                bb = i % 2
                if i >= 2:
                    sync.wait_ge(cmp, i - 1)
                sync.dma_start(out=cst_f[:rr, bb, :cdim], in_=src).then_inc(ld, 16)
                nld += 16
                sync.wait_ge(cmp, i + 1)
                sync.dma_start(out=dst, in_=cst_b[:rr, bb, :cdim]).then_inc(st, 16)
                nst += 16
            ti = 0
            for (qw, zz, ss, wout, ntiles, pc) in deq_jobs:
                lc = pc * 8
                nrows = qw.shape[0]
                for t in range(ntiles):
                    bb = ti % 2
                    rr = min(128, nrows - t * 128)
                    if ti >= 2:
                        sync.wait_ge(cmp, NC_JOBS + ti - 1)
                    rows = slice(t * 128, t * 128 + rr)
                    sync.dma_start(out=qw_sb[:rr, bb, :pc], in_=qw[rows, :]).then_inc(ld, 16)
                    sync.dma_start(out=zw_sb[:rr, bb, :pc], in_=zz[rows, :]).then_inc(ld, 16)
                    sync.dma_start(out=fs[:rr, bb, :lc], in_=ss[rows, :]).then_inc(ld, 16)
                    nld += 48
                    sync.wait_ge(cmp, NC_JOBS + ti + 1)
                    sync.dma_start(out=wout[rows, :], in_=ob[:rr, bb, :lc]).then_inc(st, 16)
                    nst += 16
                    ti += 1
            sync.wait_ge(st, nst)

        @block.vector
        def _(vector):
            w = 0
            for i, (src, dst, rr, cdim) in enumerate(cast_jobs):
                bb = i % 2
                w += 16
                vector.wait_ge(ld, w)
                vector.tensor_copy(cst_b[:rr, bb, :cdim], cst_f[:rr, bb, :cdim]).then_inc(cmp, 1)
            ti = 0
            for (qw, zz, ss, wout, ntiles, pc) in deq_jobs:
                lc = pc * 8
                nrows = qw.shape[0]
                for t in range(ntiles):
                    bb = ti % 2
                    rr = min(128, nrows - t * 128)
                    w += 48
                    vector.wait_ge(ld, w)
                    for s in range(8):
                        sh = 4 * REV[s]
                        vector.tensor_scalar(niw[:rr, s:lc:8], qw_sb[:rr, bb, :pc], sh, 15,
                                             op0=ALU.logical_shift_right, op1=ALU.bitwise_and)
                        vector.tensor_scalar(niz[:rr, s:lc:8], zw_sb[:rr, bb, :pc], sh, 15,
                                             op0=ALU.logical_shift_right, op1=ALU.bitwise_and)
                    vector.tensor_copy(fw[:rr, :lc], niw[:rr, :lc])
                    vector.tensor_copy(fz[:rr, :lc], niz[:rr, :lc])
                    vector.tensor_sub(fw[:rr, :lc], fw[:rr, :lc], fz[:rr, :lc])
                    vector.tensor_mul(ob[:rr, bb, :lc], fw[:rr, :lc], fs[:rr, bb, :lc]).then_inc(cmp, 1)
                    ti += 1

    return nc


def run_weights(inputs):
    tr = lambda a: np.ascontiguousarray(a.T)
    full = {
        "wqt": tr(inputs["L0_attn_Wq"]), "wkt": tr(inputs["L0_attn_Wk"]),
        "wvt": tr(inputs["L0_attn_Wv"]), "wot": tr(inputs["L0_attn_Wo"]),
        "l1wt": tr(inputs["L1_ffn_W"]), "l1at": tr(inputs["L1_ffn_A"]),
        "l1bt": tr(inputs["L1_ffn_B"]), "l1gwt": tr(inputs["L1_ffn_gate_w"]),
    }
    gqf = inputs["L0_ffn_gate_qweight"]
    gzf = np.repeat(inputs["L0_ffn_gate_qzeros"], 128, axis=0)
    gsf = np.repeat(inputs["L0_ffn_gate_scales"], 128, axis=0)
    uqf = inputs["L0_ffn_up_qweight"]
    uzf = np.repeat(inputs["L0_ffn_up_qzeros"], 128, axis=0)
    usf = np.repeat(inputs["L0_ffn_up_scales"], 128, axis=0)
    dqf = inputs["L0_ffn_down_qweight"]
    dzf = np.repeat(inputs["L0_ffn_down_qzeros"], 128, axis=0)
    dsf = np.repeat(inputs["L0_ffn_down_scales"], 128, axis=0)

    in_maps = []
    for c in range(NCORES):
        m = {}
        for name in ("wqt", "wkt", "wvt", "wot", "l1wt", "l1at", "l1gwt"):
            m[name] = np.ascontiguousarray(full[name][c * 512:(c + 1) * 512])
        m["l1bt"] = np.ascontiguousarray(full["l1bt"][c * 8:(c + 1) * 8])
        m["gq"] = np.ascontiguousarray(gqf[:, c * 172:(c + 1) * 172])
        m["gz"] = np.ascontiguousarray(gzf[:, c * 172:(c + 1) * 172])
        m["gs"] = np.ascontiguousarray(gsf[:, c * 1376:(c + 1) * 1376])
        m["uq"] = np.ascontiguousarray(uqf[:, c * 172:(c + 1) * 172])
        m["uz"] = np.ascontiguousarray(uzf[:, c * 172:(c + 1) * 172])
        m["us"] = np.ascontiguousarray(usf[:, c * 1376:(c + 1) * 1376])
        m["dq"] = np.ascontiguousarray(dqf[c * 1376:(c + 1) * 1376])
        m["dz"] = np.ascontiguousarray(dzf[c * 1376:(c + 1) * 1376])
        m["ds"] = np.ascontiguousarray(dsf[c * 1376:(c + 1) * 1376])
        in_maps.append(m)

    res = run_bass_kernel_spmd(build_weights(), in_maps, core_ids=list(range(NCORES)))
    out = {}
    for name in ("wqt", "wkt", "wvt", "wot", "l1wt", "l1at", "l1gwt"):
        out[name] = np.concatenate([res.results[c][name + "_bf"] for c in range(NCORES)], 0)
    out["l1bt"] = np.concatenate([res.results[c]["l1bt_bf"] for c in range(NCORES)], 0)
    out["wg"] = np.concatenate([res.results[c]["wg_bf"] for c in range(NCORES)], 1)
    out["wu"] = np.concatenate([res.results[c]["wu_bf"] for c in range(NCORES)], 1)
    out["wd"] = np.concatenate([res.results[c]["wd_bf"] for c in range(NCORES)], 0)
    return out



def build_prep(stage=4):
    nc = bass.Bass("TRN2", target_bir_lowering=False, detect_race_conditions=False)
    x_in = nc.dram_tensor("x", [1024, D], F32, kind="ExternalInput").ap()
    wqt = nc.dram_tensor("wqt_bf", [D, D], BF16, kind="ExternalInput").ap()
    wkt = nc.dram_tensor("wkt_bf", [D, 1024], BF16, kind="ExternalInput").ap()
    wvt = nc.dram_tensor("wvt_bf", [D, 1024], BF16, kind="ExternalInput").ap()
    cos_t = nc.dram_tensor("cos_t", [128, 1024], F32, kind="ExternalInput").ap()
    sin_t = nc.dram_tensor("sin_t", [128, 1024], F32, kind="ExternalInput").ap()
    ident = nc.dram_tensor("ident", [128, 128], BF16, kind="ExternalInput").ap()
    rpermt = nc.dram_tensor("rpermt", [128, 128], BF16, kind="ExternalInput").ap()

    xn_out = nc.dram_tensor("xn", [1024, D], BF16, kind="ExternalOutput").ap()
    mv_dbg = nc.dram_tensor("mv_dbg", [8, 128, 8], F32, kind="ExternalOutput").ap()
    qrt_out = nc.dram_tensor("qrt", [D, 1024], BF16, kind="ExternalOutput").ap()
    krt_out = nc.dram_tensor("krt", [1024, 1024], BF16, kind="ExternalOutput").ap()
    vt_out = nc.dram_tensor("vt", [1024, 1024], BF16, kind="ExternalOutput").ap()

    jobs = []
    if stage >= 3:
        for hc in range(32):
            jobs.append((wqt, qrt_out, hc, stage >= 4))
        for hc in range(8):
            jobs.append((wkt, krt_out, hc, stage >= 4))
        for hc in range(8):
            jobs.append((wvt, vt_out, hc, False))
    NJ = len(jobs)
    NTP = 32 if stage >= 2 else 0

    with ExitStack() as es:
        E = es.enter_context
        x_sb = E(nc.sbuf_tensor("x_sb", [128, D], F32))
        bn = E(nc.sbuf_tensor("bn", [128, 6], F32))
        mv = E(nc.sbuf_tensor("mv", [128, 8], F32))
        xn_sb = E(nc.sbuf_tensor("xn_sb", [128, D], BF16))
        xnt = E(nc.sbuf_tensor("xnt", [128, 32, 1024], BF16))
        id_sb = E(nc.sbuf_tensor("id_sb", [128, 128], BF16))
        cos_sb = E(nc.sbuf_tensor("cos_sb", [128, 1024], F32))
        sin_sb = E(nc.sbuf_tensor("sin_sb", [128, 1024], F32))
        wbuf = E(nc.sbuf_tensor("wbuf", [128, 2, 32, 128], BF16))
        qtb = E(nc.sbuf_tensor("qtb", [128, 1024], BF16))
        rperm = E(nc.sbuf_tensor("rperm", [128, 128], BF16))
        rt1 = E(nc.sbuf_tensor("rt1", [128, 1024], F32))
        rt2 = E(nc.sbuf_tensor("rt2", [128, 1024], F32))
        prt = E(nc.sbuf_tensor("prt", [128, 2, 1024], BF16))
        ps_t0 = E(nc.psum_tensor("ps_t0", [128, 128], BF16))
        ps_t1 = E(nc.psum_tensor("ps_t1", [128, 128], BF16))
        ps_t = (ps_t0, ps_t1)
        ps_p = E(nc.psum_tensor("ps_p", [128, 1024], F32))
        rot_ps = E(nc.psum_tensor("rot_ps", [128, 1024], F32))
        ld = E(nc.semaphore("ld"))
        wld = E(nc.semaphore("wld"))
        tps = E(nc.semaphore("tps"))
        tpe = E(nc.semaphore("tpe"))
        lns = E(nc.semaphore("lns"))
        lna = E(nc.semaphore("lna"))
        vcs = E(nc.semaphore("vcs"))
        mm = E(nc.semaphore("mm"))
        mmr = E(nc.semaphore("mmr"))
        evq = E(nc.semaphore("evq"))
        ev = E(nc.semaphore("ev"))
        st = E(nc.semaphore("st"))
        block = E(nc.Block())

        @block.sync
        def _(sync):
            nst = 0
            sync.dma_start(out=id_sb[:], in_=ident).then_inc(ld, 16)
            sync.dma_start(out=cos_sb[:], in_=cos_t).then_inc(ld, 16)
            sync.dma_start(out=sin_sb[:], in_=sin_t).then_inc(ld, 16)
            sync.dma_start(out=rperm[:], in_=rpermt).then_inc(ld, 16)
            for tc in range(8):
                sync.dma_start(out=x_sb[:], in_=x_in[tc * 128:(tc + 1) * 128, :]).then_inc(ld, 16)
                sync.wait_ge(lna, 4 * tc + 4)
                sync.dma_start(out=mv_dbg[tc], in_=mv[:]).then_inc(st, 16)
                nst += 16
                sync.dma_start(out=xn_out[tc * 128:(tc + 1) * 128, :], in_=xn_sb[:]).then_inc(st, 16)
                nst += 16
                sync.wait_ge(st, nst)
            for n, (wt, out_ap, hc, rope) in enumerate(jobs):
                sync.wait_ge(ev, n + 1)
                sync.dma_start(out=out_ap[hc * 128:(hc + 1) * 128, :],
                               in_=prt[:, n % 2, :]).then_inc(st, 16)
                nst += 16
                if n >= 1:
                    sync.wait_ge(st, nst - 16)
            sync.wait_ge(st, nst)

        @block.gpsimd
        def _(gpsimd):
            for n, (wt, out_ap, hc, rope) in enumerate(jobs):
                if n >= 2:
                    gpsimd.wait_ge(mm, n - 1)
                for dc in range(32):
                    gpsimd.dma_start(out=wbuf[:, n % 2, dc, :],
                                     in_=wt[dc * 128:(dc + 1) * 128, hc * 128:(hc + 1) * 128]
                                     ).then_inc(wld, 16)

        @block.tensor
        def _(tensor):
            for tc in range(8):
                for dc in range(NTP):
                    gidx = tc * 32 + dc
                    tensor.wait_ge(lna, 4 * tc + 4)
                    if gidx >= 2:
                        tensor.wait_ge(tpe, gidx - 1)
                    tensor.transpose(ps_t[gidx % 2][:],
                                     xn_sb[:, dc * 128:(dc + 1) * 128], id_sb[:]).then_inc(tps, 1)
            for n, (wt, out_ap, hc, rope) in enumerate(jobs):
                tensor.wait_ge(wld, (n + 1) * 512)
                if n >= 1:
                    tensor.wait_ge(ev, n)
                for half in range(2):
                    sl = slice(half * 512, half * 512 + 512)
                    for dc in range(32):
                        mmi = tensor.matmul(ps_p[:, sl], wbuf[:, n % 2, dc, :],
                                            xnt[:, dc, sl],
                                            start=(dc == 0), stop=(dc == 31))
                mmi.then_inc(mm, 1)
                if rope:
                    tensor.wait_ge(evq, n + 1)
                    tensor.matmul(rot_ps[:, 0:512], rperm[:], qtb[:, 0:512],
                                  start=True, stop=True)
                    tensor.matmul(rot_ps[:, 512:1024], rperm[:], qtb[:, 512:1024],
                                  start=True, stop=True).then_inc(mmr, 1)

        @block.vector
        def _(vector):
            vch = [0]

            def vstep(instr):
                vch[0] += 1
                instr.then_inc(vcs, 1)
                vector.wait_ge(vcs, vch[0])

            for tc in range(8):
                vector.wait_ge(lna, 4 * tc + 2)
                vstep(vector.tensor_scalar_mul(mv[:, 2:3], mv[:, 0:1], 1.0 / 4096.0))
                vstep(vector.tensor_scalar_mul(mv[:, 3:4], mv[:, 1:2], 1.0 / 4096.0))
                vstep(vector.tensor_mul(mv[:, 1:2], mv[:, 2:3], mv[:, 2:3]))
                vstep(vector.tensor_sub(mv[:, 3:4], mv[:, 3:4], mv[:, 1:2]))
                vector.tensor_scalar_add(mv[:, 3:4], mv[:, 3:4], 1e-5).then_inc(lns, 1)
                vector.wait_ge(lna, 4 * tc + 3)
                vstep(vector.reciprocal(mv[:, 5:6], mv[:, 4:5]))
                vstep(vector.tensor_mul(mv[:, 6:7], mv[:, 2:3], mv[:, 5:6]))
                vector.tensor_scalar_mul(mv[:, 6:7], mv[:, 6:7], -1.0).then_inc(lns, 1)
                for dc in range(NTP):
                    gidx = tc * NTP + dc
                    vector.wait_ge(tps, gidx + 1)
                    vector.tensor_copy(xnt[:, dc, tc * 128:(tc + 1) * 128],
                                       ps_t[gidx % 2][:]).then_inc(tpe, 1)
            nrope = 0
            for n, (wt, out_ap, hc, rope) in enumerate(jobs):
                vector.wait_ge(mm, n + 1)
                if n >= 2:
                    vector.wait_ge(st, 8 * 16 + (n - 1) * 16)
                if rope:
                    nrope += 1
                    vector.tensor_copy(qtb[:], ps_p[:]).then_inc(evq, 1)
                    vector.tensor_mul(rt1[:], qtb[:], cos_sb[:])
                    vector.wait_ge(mmr, nrope)
                    vector.tensor_mul(rt2[:], rot_ps[:], sin_sb[:])
                    vector.tensor_add(rt1[:], rt1[:], rt2[:])
                    vector.tensor_copy(prt[:, n % 2, :], rt1[:]).then_inc(ev, 1)
                else:
                    vector.tensor_copy(prt[:, n % 2, :], ps_p[:]).then_inc(ev, 1)

        @block.scalar
        def _(scalar):
            for tc in range(8):
                scalar.wait_ge(ld, 64 + (tc + 1) * 16)
                if tc >= 1:
                    scalar.wait_ge(tps, tc * NTP)
                    scalar.wait_ge(st, 16 * tc)
                scalar.memzero(mv[:, 0:2])
                scalar.activation(xn_sb[:], x_sb[:], AF.Copy,
                                  accum_out=mv[:, 0:1]).then_inc(lna, 1)
                scalar.wait_ge(lna, 4 * tc + 1)
                scalar.activation(xn_sb[:], x_sb[:], AF.Square,
                                  accum_out=mv[:, 1:2]).then_inc(lna, 1)
                scalar.wait_ge(lns, 2 * tc + 1)
                scalar.activation(mv[:, 4:5], mv[:, 3:4], AF.Sqrt).then_inc(lna, 1)
                scalar.wait_ge(lns, 2 * tc + 2)
                scalar.wait_ge(lna, 4 * tc + 2)
                scalar.activation(xn_sb[:], x_sb[:], AF.Identity,
                                  bias=mv[:, 6:7], scale=mv[:, 5:6]).then_inc(lna, 1)

    return nc


def make_rope_tables():
    inv_freq = 1.0 / (10000.0 ** (np.arange(0, 128, 2, dtype=np.float64) / 128.0))
    t = np.arange(S, dtype=np.float64)
    freqs = np.outer(t, inv_freq)
    emb = np.concatenate([freqs, freqs], axis=-1)
    return np.cos(emb).astype(np.float32), np.sin(emb).astype(np.float32)


def run_prep(inputs, wts):
    cos, sin = make_rope_tables()
    x_full = np.asarray(inputs["L0_attn_hidden_states"]).reshape(B * S, D)
    ident = np.eye(128, dtype=BF)
    R = np.zeros((128, 128), np.float32)
    for hd in range(64):
        R[hd, hd + 64] = -1.0
        R[hd + 64, hd] = 1.0
    rpermt = np.ascontiguousarray(R.T).astype(BF)
    in_maps = []
    for c in range(NCORES):
        rows = rows_of_core(c)
        pos = rows % S
        m = dict(
            x=np.ascontiguousarray(x_full[rows]),
            wqt_bf=wts["wqt"], wkt_bf=wts["wkt"], wvt_bf=wts["wvt"],
            cos_t=np.ascontiguousarray(cos[pos].T), sin_t=np.ascontiguousarray(sin[pos].T),
            ident=ident, rpermt=rpermt,
        )
        in_maps.append(m)
    res = run_bass_kernel_spmd(build_prep(), in_maps, core_ids=list(range(NCORES)))
    return [res.results[c] for c in range(NCORES)]



class _Plan:

    def __init__(self, sems):
        self.sems = sems
        self.cnt = {k: 0 for k in sems}
        self.steps = []

    def step(self, engine, fn, waits=(), inc=None, inc_n=1):
        w = [(s, self.cnt[s] if v is None else v) for (s, v) in waits]
        if inc is not None:
            self.cnt[inc] += inc_n
        self.steps.append((engine, fn, w, (inc, inc_n) if inc else None))

    def mark(self, sem):
        return self.cnt[sem]

    def emit(self, engine_name, eng):
        for (e, fn, waits, inc) in self.steps:
            if e != engine_name:
                continue
            for (s, v) in waits:
                eng.wait_ge(self.sems[s], v)
            instr = fn(eng)
            if inc is not None:
                (s, n) = inc
                if instr is None:
                    instr = eng.nop()
                instr.then_inc(self.sems[s], n)


def build_attn():
    nc = bass.Bass("TRN2", target_bir_lowering=False, detect_race_conditions=False)
    qrt = nc.dram_tensor("qrt", [D, 1024], BF16, kind="ExternalInput").ap()
    krf = nc.dram_tensor("krf", [B, 1024, S], BF16, kind="ExternalInput").ap()
    vf = nc.dram_tensor("vf", [B, S, 1024], BF16, kind="ExternalInput").ap()
    masks = nc.dram_tensor("masks", [8, 8, 128, 128], F32, kind="ExternalInput").ap()
    ident = nc.dram_tensor("ident", [128, 128], BF16, kind="ExternalInput").ap()
    att_out = nc.dram_tensor("att", [1024, D], BF16, kind="ExternalOutput").ap()

    with ExitStack() as es:
        E = es.enter_context
        kr_sb = E(nc.sbuf_tensor("kr_sb", [128, S], BF16))
        v_sb = E(nc.sbuf_tensor("v_sb", [128, 32, 132], BF16))
        mask_sb = E(nc.sbuf_tensor("mask_sb", [128, 8, 128], F32))
        id_sb = E(nc.sbuf_tensor("id_sb", [128, 128], BF16))
        qr_sb = E(nc.sbuf_tensor("qr_sb", [128, 4, 128], BF16))
        s4 = E(nc.sbuf_tensor("s4", [128, 16, S], BF16))
        st8 = E(nc.sbuf_tensor("st8", [128, 8, 16], F32))
        gei = E(nc.sbuf_tensor("gei", [128, 2, 16], I32))
        dump = E(nc.sbuf_tensor("dump", [128, S], BF16))
        taue = E(nc.sbuf_tensor("taue", [128, 16], F32))
        e_sb = E(nc.sbuf_tensor("e_sb", [128, S], BF16))
        E_sb = E(nc.sbuf_tensor("E_sb", [128, S], BF16))
        et_sb = E(nc.sbuf_tensor("et_sb", [128, 32, 128], BF16))
        att_sb = E(nc.sbuf_tensor("att_sb", [128, 2, 128], BF16))
        rc_sb = E(nc.sbuf_tensor("rc_sb", [128, 2], F32))
        ps_qk0 = E(nc.psum_tensor("ps_qk0", [128, 512], F32))
        ps_qk1 = E(nc.psum_tensor("ps_qk1", [128, 512], F32))
        ps_tp0 = E(nc.psum_tensor("ps_tp0", [128, 128], BF16))
        ps_tp1 = E(nc.psum_tensor("ps_tp1", [128, 128], BF16))
        ps_av0 = E(nc.psum_tensor("ps_av0", [128, 132], F32))
        ps_av1 = E(nc.psum_tensor("ps_av1", [128, 132], F32))
        ps_qk = (ps_qk0, ps_qk1)
        ps_tp = (ps_tp0, ps_tp1)
        ps_av = (ps_av0, ps_av1)
        sems = {}
        for name in ("ld", "qk", "sev", "cnt", "upd", "vch", "sch", "expd",
                     "stt", "tps", "tpe", "avm", "nrm", "sto"):
            sems[name] = E(nc.semaphore(name))
        block = E(nc.Block())

        P = _Plan(sems)
        SC = P.step
        NEG_CONST = {}
        for s in range(4):
            NEG_CONST[s] = float(SLOT_W[s] * 128 - 2 * TOPK)

        SC("sync", lambda e: e.dma_start(out=id_sb[:], in_=ident), inc="ld", inc_n=16)

        SC("vector", lambda e: e.memset(v_sb[:, :, 128:132], 0.0),
           waits=[("ld", None)], inc="vch")
        SC("vector", lambda e: e.memset(v_sb[:, :, 128:129], 1.0), inc="vch")

        par_qk = [0]
        par_tp = [0]
        par_av = [0]
        seg_evacs = [0]

        for b in range(B):
            for kvh in range(8):
                g = b * 8 + kvh
                SC("sync", lambda e, b=b, kvh=kvh: e.dma_start(
                    out=kr_sb[:], in_=krf[b, kvh * 128:(kvh + 1) * 128, :]),
                    waits=[("qk", None), ("avm", None)], inc="ld", inc_n=16)
                for kc in range(32):
                    SC("sync", lambda e, b=b, kvh=kvh, kc=kc: e.dma_start(
                        out=v_sb[:, kc, 0:128],
                        in_=vf[b, kc * 128:(kc + 1) * 128, kvh * 128:(kvh + 1) * 128]),
                        inc="ld", inc_n=16)
                ld_group = P.mark("ld")

                for s in range(4):
                    W = SLOT_W[s]
                    sg = b * 4 + s
                    SC("sync", lambda e, sg=sg: e.dma_start(
                        out=mask_sb[:], in_=masks[sg].rearrange("t p m -> p t m")),
                        waits=[("sev", None)], inc="ld", inc_n=16)
                    for qh in range(4):
                        h = kvh * 4 + qh
                        SC("sync", lambda e, h=h, sg=sg, qh=qh: e.dma_start(
                            out=qr_sb[:, qh, :],
                            in_=qrt[h * 128:(h + 1) * 128, sg * 128:(sg + 1) * 128]),
                            waits=[("qk", None)], inc="ld", inc_n=16)
                        ld_q = P.mark("ld")
                        col = s * 4 + qh
                        nseg = W * 128 // 512
                        for seg in range(nseg):
                            pq = par_qk[0]
                            par_qk[0] ^= 1
                            SC("tensor", lambda e, pq=pq, qh=qh, seg=seg: e.matmul(
                                ps_qk[pq][:], qr_sb[:, qh, :],
                                kr_sb[:, seg * 512:(seg + 1) * 512],
                                start=True, stop=True),
                                waits=[("ld", ld_q), ("sev", max(0, seg_evacs[0] - 1))],
                                inc="qk")
                            tail0 = (W - 8) * 128 // 512
                            if seg >= tail0:
                                t0 = (seg - tail0) * 4
                                SC("vector", lambda e, pq=pq, col=col, seg=seg, t0=t0:
                                    e.tensor_add(
                                        s4[:, col, seg * 512:(seg + 1) * 512],
                                        ps_qk[pq][:],
                                        mask_sb[:, t0:t0 + 4, :].rearrange("p t m -> p (t m)")),
                                    waits=[("qk", None), ("ld", None)], inc="sev")
                            else:
                                SC("vector", lambda e, pq=pq, col=col, seg=seg:
                                    e.tensor_copy(s4[:, col, seg * 512:(seg + 1) * 512],
                                                  ps_qk[pq][:]),
                                    waits=[("qk", None)], inc="sev")
                            seg_evacs[0] += 1

                sev_done = P.mark("sev")
                SC("vector", lambda e: e.memset(st8[:, 0, :], BIS_LO),
                   waits=[("sev", sev_done)], inc="vch")
                SC("vector", lambda e: e.memset(st8[:, 1, :], BIS_HI), inc="vch")
                for it in range(BIS_ITERS):
                    SC("vector", lambda e: e.tensor_add(st8[:, 2, :], st8[:, 0, :], st8[:, 1, :]),
                       waits=[("upd", None), ("vch", None)], inc="vch")
                    SC("vector", lambda e: e.tensor_scalar_mul(st8[:, 2, :], st8[:, 2, :], 0.5),
                       waits=[("vch", None)], inc="vch")
                    mid_done = P.mark("vch")
                    SC("scalar", lambda e: e.memzero(st8[:, 3, :]),
                       waits=[("vch", mid_done), ("upd", None)], inc="sch")
                    for col in range(16):
                        Wc = SLOT_W[col // 4] * 128
                        SC("scalar", lambda e, col=col, Wc=Wc: e.activation(
                            dump[:, 0:Wc], s4[:, col, 0:Wc], AF.Sign,
                            bias=st8[:, 2, col:col + 1], scale=-1.0,
                            accum_out=st8[:, 3, col:col + 1]),
                            inc="cnt")
                    cnt_done = P.mark("cnt")
                    for s in range(4):
                        SC("vector", lambda e, s=s: e.tensor_scalar(
                            gei[:, 0, s * 4:(s + 1) * 4], st8[:, 3, s * 4:(s + 1) * 4],
                            NEG_CONST[s], None, op0=ALU.is_le),
                            waits=[("cnt", cnt_done)], inc="vch")
                        SC("vector", lambda e, s=s: e.tensor_scalar(
                            gei[:, 1, s * 4:(s + 1) * 4], st8[:, 3, s * 4:(s + 1) * 4],
                            NEG_CONST[s], None, op0=ALU.is_gt),
                            inc="vch")
                    cmp_done = P.mark("vch")
                    SC("vector", lambda e: e.copy_predicated(st8[:, 0, :], gei[:, 0, :], st8[:, 2, :]),
                       waits=[("vch", cmp_done)], inc="vch")
                    SC("vector", lambda e: e.copy_predicated(st8[:, 1, :], gei[:, 1, :], st8[:, 2, :]),
                       waits=[("vch", None)], inc="upd")

                upd_done = P.mark("upd")
                SC("scalar", lambda e: e.activation(taue[:], st8[:, 0, :], AF.Exp, scale=SCALE),
                   waits=[("upd", upd_done)], inc="sch")
                taue_done = P.mark("sch")

                for s in range(4):
                    W = SLOT_W[s]
                    sg = b * 4 + s
                    for qh in range(4):
                        col = s * 4 + qh
                        h = kvh * 4 + qh
                        SC("scalar", lambda e, col=col, W=W: e.activation(
                            e_sb[:, 0:W * 128], s4[:, col, 0:W * 128], AF.Exp, scale=SCALE),
                            waits=[("sch", taue_done), ("stt", None)], inc="expd")
                        SC("vector", lambda e, col=col, W=W: e.scalar_tensor_tensor(
                            E_sb[:, 0:W * 128], in0=e_sb[:, 0:W * 128],
                            scalar=taue[:, col:col + 1], in1=e_sb[:, 0:W * 128],
                            op0=ALU.is_ge, op1=ALU.mult),
                            waits=[("expd", None), ("tpe", None)], inc="stt")
                        stt_done = P.mark("stt")
                        for kc in range(W):
                            pt = par_tp[0]
                            par_tp[0] ^= 1
                            SC("tensor", lambda e, pt=pt, kc=kc: e.transpose(
                                ps_tp[pt][:], E_sb[:, kc * 128:(kc + 1) * 128], id_sb[:]),
                                waits=[("stt", stt_done), ("tpe", max(0, P.cnt["tps"] - 1))],
                                inc="tps")
                            SC("vector", lambda e, pt=pt, kc=kc: e.tensor_copy(
                                et_sb[:, kc, :], ps_tp[pt][:]),
                                waits=[("tps", None)], inc="tpe")
                        tpe_done = P.mark("tpe")
                        pa = par_av[0]
                        par_av[0] ^= 1
                        for kc in range(W):
                            SC("tensor", lambda e, pa=pa, kc=kc, W=W: e.matmul(
                                ps_av[pa][:, 0:129], et_sb[:, kc, :], v_sb[:, kc, 0:129],
                                start=(kc == 0), stop=(kc == W - 1)),
                                waits=([("tpe", tpe_done), ("ld", ld_group),
                                        ("nrm", max(0, P.cnt["avm"] - 1))] if kc == 0 else []),
                                inc=("avm" if kc == W - 1 else None))
                        SC("vector", lambda e, pa=pa: e.reciprocal(
                            rc_sb[:, pa:pa + 1], ps_av[pa][:, 128:129]),
                            waits=[("avm", None)], inc="vch")
                        SC("vector", lambda e, pa=pa: e.tensor_scalar(
                            att_sb[:, pa, :], ps_av[pa][:, 0:128], rc_sb[:, pa:pa + 1],
                            None, op0=ALU.mult),
                            waits=[("vch", None), ("sto", max(0, P.cnt["nrm"] - 1))],
                            inc="nrm")
                        SC("sync", lambda e, pa=pa, sg=sg, h=h: e.dma_start(
                            out=att_out[sg * 128:(sg + 1) * 128, h * 128:(h + 1) * 128],
                            in_=att_sb[:, pa, :]),
                            waits=[("nrm", None)], inc="sto", inc_n=16)

        SC("sync", lambda e: None, waits=[("sto", None)])

        @block.sync
        def _(sync):
            P.emit("sync", sync)

        @block.tensor
        def _(tensor):
            P.emit("tensor", tensor)

        @block.vector
        def _(vector):
            P.emit("vector", vector)

        @block.scalar
        def _(scalar):
            P.emit("scalar", scalar)

    return nc


def make_masks(c):
    m = np.zeros((8, 8, 128, 128), np.float32)
    tri = np.where(np.arange(128)[:, None] >= np.arange(128)[None, :], 0.0,
                   np.float32(SENT))
    for b in range(B):
        for s in range(4):
            j = blocks_of_core(c)[s]
            Wc = SLOT_W[s]
            for t in range(8):
                ch = Wc - 8 + t
                if ch == j:
                    m[b * 4 + s, t] = tri
                elif ch > j:
                    m[b * 4 + s, t] = SENT
    return m


def run_attn(inputs, prep):
    krf = np.zeros((B, 1024, S), BF)
    vf = np.zeros((B, S, 1024), BF)
    for c in range(NCORES):
        rows = rows_of_core(c)
        for li in range(8):
            bb = li // 4
            j = blocks_of_core(c)[li % 4]
            sl = slice(j * 128, (j + 1) * 128)
            lsl = slice(li * 128, (li + 1) * 128)
            krf[bb, :, sl] = prep[c]["krt"][:, lsl]
            vf[bb, sl, :] = prep[c]["vt"][:, lsl].T
    in_maps = []
    for c in range(NCORES):
        in_maps.append(dict(qrt=prep[c]["qrt"], krf=krf, vf=vf,
                            masks=make_masks(c), ident=np.eye(128, dtype=BF)))
    res = run_bass_kernel_spmd(build_attn(), in_maps, core_ids=list(range(NCORES)))
    return [res.results[c]["att"] for c in range(NCORES)]



ICB = 3
NICG = (ICH + ICB - 1) // ICB


def build_ffn(stage=3):
    nc = bass.Bass("TRN2", target_bir_lowering=False, detect_race_conditions=False)
    attnt = nc.dram_tensor("attnt", [D, 1024], BF16, kind="ExternalInput").ap()
    xnt_in = nc.dram_tensor("xnt_in", [D, 1024], BF16, kind="ExternalInput").ap()
    x_in = nc.dram_tensor("x", [1024, D], F32, kind="ExternalInput").ap()
    wg = nc.dram_tensor("wg_bf", [D, I_FFN], BF16, kind="ExternalInput").ap()
    wu = nc.dram_tensor("wu_bf", [D, I_FFN], BF16, kind="ExternalInput").ap()
    wd = nc.dram_tensor("wd_bf", [I_FFN, D], BF16, kind="ExternalInput").ap()
    wot = nc.dram_tensor("wot_bf", [D, D], BF16, kind="ExternalInput").ap()
    x1_out = nc.dram_tensor("x1", [1024, D], F32, kind="ExternalOutput").ap()
    h_buf = nc.dram_tensor("h_buf", [ICH, 128, 1024], BF16,
                           kind="Internal" if stage >= 2 else "ExternalOutput").ap()

    with ExitStack() as es:
        E = es.enter_context
        xnt = E(nc.sbuf_tensor("xnt", [128, 32, 1024], BF16))
        arena = E(nc.sbuf_tensor("arena", [128, 45056], BF16))
        wot_sb = E(nc.sbuf_tensor("wot_sb", [128, 32, 512], BF16))
        sg_sb = E(nc.sbuf_tensor("sg_sb", [128, 1024], BF16))
        h_sb = E(nc.sbuf_tensor("h_sb", [128, 2, 1024], BF16))
        hr_sb = E(nc.sbuf_tensor("hr_sb", [128, 8, 128], BF16))
        x_sb = E(nc.sbuf_tensor("x_sb", [128, 2, 512], F32))
        x1_sb = E(nc.sbuf_tensor("x1_sb", [128, 2, 512], F32))
        scr_sb = E(nc.sbuf_tensor("scr_sb", [128, 64], F32))
        ps_g = E(nc.psum_tensor("ps_g", [128, 1024], F32))
        ps_u = E(nc.psum_tensor("ps_u", [128, 1024], F32))
        ps_o0 = E(nc.psum_tensor("ps_o0", [128, 512], F32))
        ps_o1 = E(nc.psum_tensor("ps_o1", [128, 512], F32))
        ps_o = (ps_o0, ps_o1)
        sems = {}
        for name in ("ld", "wldA", "wldB", "hld", "hcons", "mmg", "mmu", "sil", "hmul",
                     "hst", "mmo", "ev", "sto", "xld"):
            sems[name] = E(nc.semaphore(name))
        block = E(nc.Block())

        P = _Plan(sems)
        SC = P.step

        for dc in range(32):
            SC("sync", lambda e, dc=dc: e.dma_start(
                out=xnt[:, dc, :], in_=xnt_in[dc * 128:(dc + 1) * 128, :]),
                inc="ld", inc_n=16)
        ld_xnt = P.mark("ld")

        wg_v = arena[:, 0:32 * ICB * 128].rearrange("p (dc c) -> p dc c", dc=32)
        wu_v = arena[:, 32 * ICB * 128:64 * ICB * 128].rearrange("p (dc c) -> p dc c", dc=32)

        for icg in range(NICG):
            nic = min(ICB, ICH - icg * ICB)
            cw = nic * 128
            for dc in range(32):
                SC("gpsimd", lambda e, icg=icg, dc=dc, cw=cw: e.dma_start(
                    out=wg_v[:, dc, 0:cw],
                    in_=wg[dc * 128:(dc + 1) * 128, icg * ICB * 128:icg * ICB * 128 + cw]),
                    waits=([("mmu", None)] if dc == 0 else []), inc="wldA", inc_n=16)
                SC("gpsimd", lambda e, icg=icg, dc=dc, cw=cw: e.dma_start(
                    out=wu_v[:, dc, 0:cw],
                    in_=wu[dc * 128:(dc + 1) * 128, icg * ICB * 128:icg * ICB * 128 + cw]),
                    inc="wldA", inc_n=16)
            wld_done = P.mark("wldA")
            for ici in range(nic):
                ic = icg * ICB + ici
                hp = ic % 2
                for half in range(2):
                    sl = slice(half * 512, half * 512 + 512)
                    csl = slice(ici * 128 + 0, ici * 128 + 128)
                    for dc in range(32):
                        SC("tensor", lambda e, dc=dc, sl=sl, csl=csl: e.matmul(
                            ps_g[:, sl], wg_v[:, dc, csl], xnt[:, dc, sl],
                            start=(dc == 0), stop=(dc == 31)),
                            waits=([("wldA", wld_done), ("ld", ld_xnt),
                                    ("sil", None)] if (half == 0 and dc == 0) else []),
                            inc=("mmg" if (half == 1 and dc == 31) else None))
                for half in range(2):
                    sl = slice(half * 512, half * 512 + 512)
                    csl = slice(ici * 128 + 0, ici * 128 + 128)
                    for dc in range(32):
                        SC("tensor", lambda e, dc=dc, sl=sl, csl=csl: e.matmul(
                            ps_u[:, sl], wu_v[:, dc, csl], xnt[:, dc, sl],
                            start=(dc == 0), stop=(dc == 31)),
                            waits=([("hmul", None)] if (half == 0 and dc == 0) else []),
                            inc=("mmu" if (half == 1 and dc == 31) else None))
                SC("scalar", lambda e: e.activation(sg_sb[:], ps_g[:], AF.Silu),
                   waits=[("mmg", None), ("hmul", None)], inc="sil")
                SC("vector", lambda e, hp=hp: e.tensor_mul(h_sb[:, hp, :], ps_u[:], sg_sb[:]),
                   waits=[("mmu", None), ("sil", None),
                          ("hst", max(0, (P.cnt["hmul"] - 1) * 16))],
                   inc="hmul")
                SC("sync", lambda e, ic=ic, hp=hp: e.dma_start(
                    out=h_buf[ic], in_=h_sb[:, hp, :]),
                    waits=[("hmul", None)], inc="hst", inc_n=16)

        hst_all = P.mark("hst")
        if stage < 2:
            SC("sync", lambda e: e.memset(x1_sb[:, 0, :], 0.0) if False else None,
               waits=[("hst", hst_all)])
            SC("sync", lambda e: e.dma_start(out=x1_out[0:128, :],
                                             in_=x_sb[:, 0, 0:512]) if False else None)
        for dc in range(32 if stage >= 2 else 0):
            SC("sync", lambda e, dc=dc: e.dma_start(
                out=xnt[:, dc, :], in_=attnt[dc * 128:(dc + 1) * 128, :]),
                waits=([("mmu", None), ("hst", hst_all)] if dc == 0 else []),
                inc="ld", inc_n=16)
        ld_att = P.mark("ld")
        wd_v = arena[:, 0:ICH * 512].rearrange("p (i c) -> p i c", i=ICH)
        nhr = [0]
        for oc in range(_NOC[0] if stage >= 2 else 0):
            for ic in range(ICH if stage >= 3 else 0):
                SC("gpsimd", lambda e, oc=oc, ic=ic: e.dma_start(
                    out=wd_v[:, ic, :],
                    in_=wd[ic * 128:(ic + 1) * 128, oc * 512:(oc + 1) * 512]),
                    waits=([("mmo", None), ("mmu", None)] if ic == 0 else []),
                    inc="wldB", inc_n=16)
            for dc in range(32):
                SC("gpsimd", lambda e, oc=oc, dc=dc: e.dma_start(
                    out=wot_sb[:, dc, :],
                    in_=wot[dc * 128:(dc + 1) * 128, oc * 512:(oc + 1) * 512]),
                    waits=([("mmo", None)] if (dc == 0 and stage < 3) else []),
                    inc="wldB", inc_n=16)
            wldb_done = P.mark("wldB")
            for tc in range(8):
                po = (oc * 8 + tc) % 2
                SC("sync", lambda e, tc=tc, oc=oc, po=po: e.dma_start(
                    out=x_sb[:, po, :],
                    in_=x_in[tc * 128:(tc + 1) * 128, oc * 512:(oc + 1) * 512]),
                    waits=[("ev", max(0, P.cnt["mmo"] - 1))], inc="xld", inc_n=16)
                xld_done = P.mark("xld")
                base_hcons = P.cnt["hcons"]
                nquad = (ICH + 3) // 4
                for qi in range(nquad if stage >= 3 else 0):
                    n_in = min(4, ICH - qi * 4)
                    w = [("hcons", base_hcons + max(0, qi))]
                    if qi < 2:
                        w.append(("mmo", P.cnt["mmo"]))
                    SC("sync", lambda e, qi=qi, tc=tc, n_in=n_in: e.dma_start(
                        out=hr_sb[:, (qi % 2) * 4:(qi % 2) * 4 + n_in, :],
                        in_=h_buf[qi * 4:qi * 4 + n_in, :, tc * 128:(tc + 1) * 128
                                  ].rearrange("i p t -> p i t")),
                        waits=w, inc="hld", inc_n=16)
                for dc in range(32):
                    SC("tensor", lambda e, dc=dc, tc=tc, po=po, stage=stage: e.matmul(
                        ps_o[po][:], xnt[:, dc, tc * 128:(tc + 1) * 128], wot_sb[:, dc, :],
                        start=(dc == 0), stop=(stage < 3 and dc == 31)),
                        waits=([("wldB", wldb_done), ("ld", ld_att),
                                ("ev", max(0, P.cnt["mmo"] - 1))] if dc == 0 else []),
                        inc=("mmo" if (stage < 3 and dc == 31) else None))
                nquad = (ICH + 3) // 4
                base_hld = P.cnt["hld"] - nquad * 16
                for ic in range(ICH if stage >= 3 else 0):
                    incsem = None
                    if ic % 4 == 3:
                        incsem = "hcons"
                    if ic == ICH - 1:
                        incsem = "mmo"
                    qi = ic // 4
                    SC("tensor", lambda e, ic=ic, po=po, qi=qi: e.matmul(
                        ps_o[po][:], hr_sb[:, (qi % 2) * 4 + ic % 4, :], wd_v[:, ic, :],
                        start=False, stop=(ic == ICH - 1)),
                        waits=[("hld", base_hld + (qi + 1) * 16)],
                        inc=incsem)

                SC("vector", lambda e, po=po: e.tensor_copy(scr_sb[:], ps_o[po][:, 0:64]),
                    waits=[("mmo", None)])
                SC("vector", lambda e, po=po: e.tensor_add(
                    x1_sb[:, po, :], ps_o[po][:], x_sb[:, po, :]),
                    waits=[("xld", xld_done),
                           ("sto", max(0, (P.cnt["ev"] - 1) * 16))], inc="ev")
                SC("sync", lambda e, tc=tc, oc=oc, po=po: e.dma_start(
                    out=x1_out[tc * 128:(tc + 1) * 128, oc * 512:(oc + 1) * 512],
                    in_=x1_sb[:, po, :]),
                    waits=[("ev", None)], inc="sto", inc_n=16)

        SC("sync", lambda e: None, waits=[("sto", None)])

        @block.sync
        def _(sync):
            P.emit("sync", sync)

        @block.gpsimd
        def _(gpsimd):
            P.emit("gpsimd", gpsimd)

        @block.tensor
        def _(tensor):
            P.emit("tensor", tensor)

        @block.vector
        def _(vector):
            P.emit("vector", vector)

        @block.scalar
        def _(scalar):
            P.emit("scalar", scalar)

    return nc


def run_ffn(inputs, wts, prep, att, stage=3):
    x_full = np.asarray(inputs["L0_attn_hidden_states"]).reshape(B * S, D)
    in_maps = []
    for c in range(NCORES):
        rows = rows_of_core(c)
        in_maps.append(dict(
            attnt=np.ascontiguousarray(np.asarray(att[c]).T),
            xnt_in=np.ascontiguousarray(np.asarray(prep[c]["xn"]).T),
            x=np.ascontiguousarray(x_full[rows]),
            wg_bf=wts["wg"], wu_bf=wts["wu"], wd_bf=wts["wd"], wot_bf=wts["wot"],
        ))
    res = run_bass_kernel_spmd(build_ffn(stage), in_maps, core_ids=list(range(NCORES)))
    if stage < 2:
        return [res.results[c]["h_buf"] for c in range(NCORES)]
    return [res.results[c]["x1"] for c in range(NCORES)]



def build_ln():
    nc = bass.Bass("TRN2", target_bir_lowering=False, detect_race_conditions=False)
    x_in = nc.dram_tensor("x", [1024, D], F32, kind="ExternalInput").ap()
    xn_out = nc.dram_tensor("xn", [1024, D], BF16, kind="ExternalOutput").ap()
    with ExitStack() as es:
        E = es.enter_context
        x_sb = E(nc.sbuf_tensor("x_sb", [128, 2, D], F32))
        xn_sb = E(nc.sbuf_tensor("xn_sb", [128, 2, D], BF16))
        mv = E(nc.sbuf_tensor("mv", [128, 8], F32))
        sems = {}
        for name in ("ld", "lna", "lns", "vcs", "st"):
            sems[name] = E(nc.semaphore(name))
        block = E(nc.Block())
        P = _Plan(sems)
        SC = P.step
        for tc in range(8):
            p = tc % 2
            SC("sync", lambda e, tc=tc, p=p: e.dma_start(
                out=x_sb[:, p, :], in_=x_in[tc * 128:(tc + 1) * 128, :]),
                waits=[("lna", 4 * (tc - 1))] if tc >= 2 else [],
                inc="ld", inc_n=16)
            ld_t = P.mark("ld")
            SC("scalar", lambda e, p=p, tc=tc: e.memzero(mv[:, 0:2]),
               waits=[("ld", ld_t), ("lns", None)] + ([("st", 16 * (tc - 1))] if tc >= 2 else []))
            SC("scalar", lambda e, p=p: e.activation(
                xn_sb[:, p, :], x_sb[:, p, :], AF.Copy, accum_out=mv[:, 0:1]), inc="lna")
            SC("scalar", lambda e, p=p: e.activation(
                xn_sb[:, p, :], x_sb[:, p, :], AF.Square, accum_out=mv[:, 1:2]),
               waits=[("lna", None)], inc="lna")
            SC("vector", lambda e: e.tensor_scalar_mul(mv[:, 2:3], mv[:, 0:1], 1.0 / 4096.0),
               waits=[("lna", None)], inc="vcs")
            SC("vector", lambda e: e.tensor_scalar_mul(mv[:, 3:4], mv[:, 1:2], 1.0 / 4096.0),
               waits=[("vcs", None)], inc="vcs")
            SC("vector", lambda e: e.tensor_mul(mv[:, 1:2], mv[:, 2:3], mv[:, 2:3]),
               waits=[("vcs", None)], inc="vcs")
            SC("vector", lambda e: e.tensor_sub(mv[:, 3:4], mv[:, 3:4], mv[:, 1:2]),
               waits=[("vcs", None)], inc="vcs")
            SC("vector", lambda e: e.tensor_scalar_add(mv[:, 3:4], mv[:, 3:4], 1e-5),
               waits=[("vcs", None)], inc="lns")
            SC("scalar", lambda e: e.activation(mv[:, 4:5], mv[:, 3:4], AF.Sqrt),
               waits=[("lns", None)], inc="lna")
            SC("vector", lambda e: e.reciprocal(mv[:, 5:6], mv[:, 4:5]),
               waits=[("lna", None)], inc="vcs")
            SC("vector", lambda e: e.tensor_mul(mv[:, 6:7], mv[:, 2:3], mv[:, 5:6]),
               waits=[("vcs", None)], inc="vcs")
            SC("vector", lambda e: e.tensor_scalar_mul(mv[:, 6:7], mv[:, 6:7], -1.0),
               waits=[("vcs", None)], inc="lns")
            SC("scalar", lambda e, p=p: e.activation(
                xn_sb[:, p, :], x_sb[:, p, :], AF.Identity,
                bias=mv[:, 6:7], scale=mv[:, 5:6]),
               waits=[("lns", None)], inc="lna")
            SC("sync", lambda e, tc=tc, p=p: e.dma_start(
                out=xn_out[tc * 128:(tc + 1) * 128, :], in_=xn_sb[:, p, :]),
                waits=[("lna", None)], inc="st", inc_n=16)
        SC("sync", lambda e: None, waits=[("st", None)])

        @block.sync
        def _(sync):
            P.emit("sync", sync)

        @block.vector
        def _(vector):
            P.emit("vector", vector)

        @block.scalar
        def _(scalar):
            P.emit("scalar", scalar)

    return nc


def run_ln(x1_shards):
    in_maps = [dict(x=np.ascontiguousarray(x1_shards[c])) for c in range(NCORES)]
    res = run_bass_kernel_spmd(build_ln(), in_maps, core_ids=list(range(NCORES)))
    return [res.results[c]["xn"] for c in range(NCORES)]



def build_l1():
    nc = bass.Bass("TRN2", target_bir_lowering=False, detect_race_conditions=False)
    x1_in = nc.dram_tensor("x1", [1024, D], F32, kind="ExternalInput").ap()
    xn1_in = nc.dram_tensor("xn1", [1024, D], BF16, kind="ExternalInput").ap()
    xnt_in = nc.dram_tensor("xnt", [D, 1024], BF16, kind="ExternalInput").ap()
    xntr_in = nc.dram_tensor("xntr", [D, 1024], BF16, kind="ExternalInput").ap()
    wt_in = nc.dram_tensor("l1wt_bf", [D, D], BF16, kind="ExternalInput").ap()
    at_in = nc.dram_tensor("l1at_bf", [D, 64], BF16, kind="ExternalInput").ap()
    bt_in = nc.dram_tensor("l1bt_bf", [64, D], BF16, kind="ExternalInput").ap()
    gwt_in = nc.dram_tensor("l1gwt_bf", [D, 4], BF16, kind="ExternalInput").ap()
    brow_in = nc.dram_tensor("b_row", [1, D], F32, kind="ExternalInput").ap()
    gbrow_in = nc.dram_tensor("gb_row", [1, 4], F32, kind="ExternalInput").ap()
    dv_in = nc.dram_tensor("d_val", [1, 1], F32, kind="ExternalInput").ap()
    lstep_in = nc.dram_tensor("log_step", [1, 1], F32, kind="ExternalInput").ap()
    om_re_in = nc.dram_tensor("om_re", [128, 32], F32, kind="ExternalInput").ap()
    om_im_in = nc.dram_tensor("om_im", [128, 32], F32, kind="ExternalInput").ap()
    rows8_in = nc.dram_tensor("rows8", [1, 512], F32, kind="ExternalInput").ap()
    vctre_in = nc.dram_tensor("vct_re", [64, 64], F32, kind="ExternalInput").ap()
    vctim_in = nc.dram_tensor("vct_im", [64, 64], F32, kind="ExternalInput").ap()
    bp_in = nc.dram_tensor("bp", [64, 1], F32, kind="ExternalInput").ap()
    w64re_in = nc.dram_tensor("w64_re", [64, 64], F32, kind="ExternalInput").ap()
    w64im_in = nc.dram_tensor("w64_im", [64, 64], F32, kind="ExternalInput").ap()
    w64imn_in = nc.dram_tensor("w64_imn", [64, 64], F32, kind="ExternalInput").ap()
    wfre_in = nc.dram_tensor("wf_re", [64, 64], F32, kind="ExternalInput").ap()
    wfimn_in = nc.dram_tensor("wf_imn", [64, 64], F32, kind="ExternalInput").ap()
    twre_in = nc.dram_tensor("tw_re", [64, 64], F32, kind="ExternalInput").ap()
    twim_in = nc.dram_tensor("tw_im", [64, 64], F32, kind="ExternalInput").ap()
    ones_in = nc.dram_tensor("ones_row", [1, 128], F32, kind="ExternalInput").ap()
    iota_in = nc.dram_tensor("iota64", [64, 1], F32, kind="ExternalInput").ap()
    x2_out = nc.dram_tensor("x2", [1024, D], F32, kind="ExternalOutput").ap()
    kp_buf = nc.dram_tensor("kp_buf", [8192], BF16, kind="Internal").ap()
    ar_re_buf = nc.dram_tensor("ar_re", [4096], F32, kind="Internal").ap()
    ar_im_buf = nc.dram_tensor("ar_im", [4096], F32, kind="Internal").ap()
    rs_buf = nc.dram_tensor("rs_buf", [1, 1024], F32, kind="Internal").ap()

    with ExitStack() as es:
        E = es.enter_context
        xnt = E(nc.sbuf_tensor("xnt_sb", [128, 32, 1024], BF16))
        trev = E(nc.sbuf_tensor("trev", [128, 2, 1024], BF16))
        kp_sb = E(nc.sbuf_tensor("kp_sb", [128, 32, 512], BF16))
        wt_sb = E(nc.sbuf_tensor("wt_sb", [128, 32, 512], BF16))
        bt_sb = E(nc.sbuf_tensor("bt_sb", [64, D], BF16))
        at_sb = E(nc.sbuf_tensor("at_sb", [128, 32, 64], BF16))
        gw_sb = E(nc.sbuf_tensor("gw_sb", [128, 32, 4], BF16))
        pm_sb = E(nc.sbuf_tensor("pm_sb", [64, 1024], BF16))
        msk_sb = E(nc.sbuf_tensor("msk_sb", [64, 1024], BF16))
        om_sb = E(nc.sbuf_tensor("om_sb", [128, 2, 32], F32))
        cplx = E(nc.sbuf_tensor("cplx", [128, 24, 32], F32))
        wb_sb = E(nc.sbuf_tensor("wb_sb", [128, 10, 64], F32))
        kc_sb = E(nc.sbuf_tensor("kc_sb", [128, 8, 32], F32))
        cau = E(nc.sbuf_tensor("cau", [128, 10, 64], F32))
        row_sb = E(nc.sbuf_tensor("row_sb", [1, 512], F32))
        brr_sb = E(nc.sbuf_tensor("brr_sb", [1, 192], F32))
        sm1 = E(nc.sbuf_tensor("sm1", [64, 2, 64], F32))
        cst64 = E(nc.sbuf_tensor("cst64", [64, 7, 64], F32))
        bp_sb = E(nc.sbuf_tensor("bp_sb", [64, 1], F32))
        s2_sb = E(nc.sbuf_tensor("s2_sb", [64, 128], F32))
        mt_sb = E(nc.sbuf_tensor("mt_sb", [64, 128], F32))
        kf_sb = E(nc.sbuf_tensor("kf_sb", [64, 64], BF16))
        zz_sb = E(nc.sbuf_tensor("zz_sb", [128, 32], BF16))
        ones_sb = E(nc.sbuf_tensor("ones_sb", [1, 128], F32))
        iota_sb = E(nc.sbuf_tensor("iota_sb", [64, 1], F32))
        brow_sb = E(nc.sbuf_tensor("brow_sb", [1, D], F32))
        gb_sb = E(nc.sbuf_tensor("gb_sb", [128, 4], F32))
        sc_sb = E(nc.sbuf_tensor("sc_sb", [128, 24], F32))
        rs_sb = E(nc.sbuf_tensor("rs_sb", [128, 1], F32))
        rsrow_sb = E(nc.sbuf_tensor("rsrow", [1, 1024], F32))
        sval = E(nc.sbuf_tensor("sval", [1, 8], F32))
        gbrow_sb = E(nc.sbuf_tensor("gbrow_sb", [1, 4], F32))
        db_sb = E(nc.sbuf_tensor("db_sb", [128, 2], F32))
        bb_sb = E(nc.sbuf_tensor("bb_sb", [128, 512], F32))
        xt_sb = E(nc.sbuf_tensor("xt_sb", [128, 2, 512], F32))
        ut_sb = E(nc.sbuf_tensor("ut_sb", [128, 2, 512], BF16))
        tmp_sb = E(nc.sbuf_tensor("tmp_sb", [128, 2, 512], F32))
        x2_sb = E(nc.sbuf_tensor("x2_sb", [128, 2, 512], F32))
        ps4 = [E(nc.psum_tensor(f"ps4_{i}", [128, 512], F32)) for i in range(4)]
        ps_pt = E(nc.psum_tensor("ps_pt", [64, 1024], F32))
        ps_s = E(nc.psum_tensor("ps_s", [128, 512], F32))
        ps_sc = E(nc.psum_tensor("ps_sc", [128, 8], F32))
        sems = {}
        for name in ("ld", "vch", "sch", "mmk", "evk", "wld", "tld", "mmm",
                     "ev", "sto", "xld", "scm", "arst", "kpst", "rsst", "mmp"):
            sems[name] = E(nc.semaphore(name))
        block = E(nc.Block())
        P = _Plan(sems)
        SC = P.step

        def V(fn, waits=(), inc="vch"):
            SC("vector", fn, waits=waits, inc=inc)

        def VW(fn, inc="vch"):
            SC("vector", fn, waits=[("vch", None)], inc=inc)

        for name, (dst, srcap) in {
            "om_re": (om_sb[:, 0, :], om_re_in), "om_im": (om_sb[:, 1, :], om_im_in),
            "rows8": (row_sb[:], rows8_in),
            "vct_re": (sm1[:, 0, :], vctre_in), "vct_im": (sm1[:, 1, :], vctim_in),
            "bp": (bp_sb[:], bp_in),
            "w64_re": (cst64[:, 0, :], w64re_in), "w64_im": (cst64[:, 1, :], w64im_in),
            "w64_imn": (cst64[:, 2, :], w64imn_in), "wf_re": (cst64[:, 3, :], wfre_in),
            "wf_imn": (cst64[:, 4, :], wfimn_in), "tw_re": (cst64[:, 5, :], twre_in),
            "tw_im": (cst64[:, 6, :], twim_in),
            "ones": (ones_sb[:], ones_in), "iota": (iota_sb[:], iota_in),
            "brow": (brow_sb[:], brow_in), "gbrow": (gbrow_sb[:], gbrow_in),
            "lstep": (sval[0:1, 0:1], lstep_in),
            "dval": (sval[0:1, 1:2], dv_in),
            "bt": (bt_sb[:], bt_in),
        }.items():
            SC("sync", lambda e, dst=dst, srcap=srcap: e.dma_start(out=dst, in_=srcap),
               inc="ld", inc_n=16)
        for dc in range(32):
            SC("sync", lambda e, dc=dc: e.dma_start(
                out=xnt[:, dc, :], in_=xnt_in[dc * 128:(dc + 1) * 128, :]), inc="ld", inc_n=16)
            SC("sync", lambda e, dc=dc: e.dma_start(
                out=at_sb[:, dc, :], in_=at_in[dc * 128:(dc + 1) * 128, :]), inc="ld", inc_n=16)
            SC("sync", lambda e, dc=dc: e.dma_start(
                out=gw_sb[:, dc, :], in_=gwt_in[dc * 128:(dc + 1) * 128, :]), inc="ld", inc_n=16)
        ld_all = P.mark("ld")

        SC("scalar", lambda e: e.activation(sval[0:1, 2:3], sval[0:1, 0:1], AF.Exp),
           waits=[("ld", ld_all)], inc="sch")
        V(lambda e: e.reciprocal(sval[0:1, 3:4], sval[0:1, 2:3]), waits=[("sch", None)])
        VW(lambda e: e.tensor_scalar_mul(sval[0:1, 4:5], sval[0:1, 3:4], 2.0))
        SC("tensor", lambda e: e.matmul(ps_s[:, 0:1], ones_sb[:], sval[0:1, 4:5],
                                        start=True, stop=True),
           waits=[("vch", None)], inc="mmk")
        SC("tensor", lambda e: e.matmul(ps_s[:, 1:2], ones_sb[:], sval[0:1, 1:2],
                                        start=True, stop=True), inc="mmk")
        V(lambda e: e.tensor_copy(db_sb[:, 1:2], ps_s[:, 0:1]), waits=[("mmk", None)])
        VW(lambda e: e.tensor_copy(db_sb[:, 0:1], ps_s[:, 1:2]), inc="evk")
        V(lambda e: e.tensor_scalar_add(cplx[:, 0, :], om_sb[:, 0, :], 1.0))
        V(lambda e: e.tensor_scalar(cplx[:, 2, :], om_sb[:, 0, :], -1.0, 1.0,
                                    op0=ALU.mult, op1=ALU.add))
        V(lambda e: e.tensor_scalar_mul(cplx[:, 3, :], om_sb[:, 1, :], -1.0))
        VW(lambda e: e.tensor_mul(cplx[:, 9, :], cplx[:, 0, :], cplx[:, 0, :]))
        V(lambda e: e.tensor_mul(cplx[:, 10, :], om_sb[:, 1, :], om_sb[:, 1, :]))
        VW(lambda e: e.tensor_add(cplx[:, 9, :], cplx[:, 9, :], cplx[:, 10, :]))
        VW(lambda e: e.reciprocal(cplx[:, 4, :], cplx[:, 9, :]))
        VW(lambda e: e.tensor_mul(cplx[:, 9, :], cplx[:, 2, :], cplx[:, 0, :]))
        V(lambda e: e.tensor_mul(cplx[:, 10, :], cplx[:, 3, :], om_sb[:, 1, :]))
        VW(lambda e: e.tensor_add(cplx[:, 9, :], cplx[:, 9, :], cplx[:, 10, :]))
        VW(lambda e: e.tensor_mul(cplx[:, 9, :], cplx[:, 9, :], cplx[:, 4, :]))
        SC("vector", lambda e: e.tensor_scalar(cplx[:, 5, :], cplx[:, 9, :],
                                               db_sb[:, 1:2], None, op0=ALU.mult),
           waits=[("vch", None), ("evk", None)], inc="vch")
        V(lambda e: e.tensor_mul(cplx[:, 9, :], cplx[:, 3, :], cplx[:, 0, :]))
        V(lambda e: e.tensor_mul(cplx[:, 10, :], cplx[:, 2, :], om_sb[:, 1, :]))
        VW(lambda e: e.tensor_sub(cplx[:, 9, :], cplx[:, 9, :], cplx[:, 10, :]))
        VW(lambda e: e.tensor_mul(cplx[:, 9, :], cplx[:, 9, :], cplx[:, 4, :]))
        VW(lambda e: e.tensor_scalar(cplx[:, 6, :], cplx[:, 9, :], db_sb[:, 1:2], None,
                                     op0=ALU.mult))
        VW(lambda e: e.tensor_mul(cplx[:, 7, :], cplx[:, 0, :], cplx[:, 4, :]))
        VW(lambda e: e.tensor_scalar_mul(cplx[:, 7, :], cplx[:, 7, :], 2.0))
        V(lambda e: e.tensor_mul(cplx[:, 8, :], om_sb[:, 1, :], cplx[:, 4, :]))
        VW(lambda e: e.tensor_scalar_mul(cplx[:, 8, :], cplx[:, 8, :], -2.0))
        SC("tensor", lambda e: e.matmul(ps_s[0:1, 64:128], bp_sb[:], sm1[:, 0, :],
                                        start=True, stop=True),
           waits=[("ld", ld_all), ("evk", None)], inc="mmk")
        SC("tensor", lambda e: e.matmul(ps_s[0:1, 128:192], bp_sb[:], sm1[:, 1, :],
                                        start=True, stop=True), inc="mmk")
        V(lambda e: e.tensor_copy(brr_sb[0:1, 0:64], ps_s[0:1, 64:128]),
          waits=[("mmk", None)])
        VW(lambda e: e.tensor_copy(brr_sb[0:1, 64:128], ps_s[0:1, 128:192]), inc="evk")
        wrows = {}
        widx = [0]

        def cmul_conj(a_re, a_im, b_re, b_im, out_re, out_im):
            V(lambda e: e.tensor_mul(brr_sb[0:1, 128:192][:, 0:64], a_re, b_re))
            V(lambda e: e.tensor_mul(out_im, a_re, b_im))
            VW(lambda e: e.tensor_mul(out_re, a_im, b_im))
            VW(lambda e: e.tensor_add(out_re, out_re, brr_sb[0:1, 128:192][:, 0:64]))
            V(lambda e: e.tensor_mul(brr_sb[0:1, 128:192][:, 0:64], a_im, b_re))
            VW(lambda e: e.tensor_sub(out_im, out_im, brr_sb[0:1, 128:192][:, 0:64]))

        w8 = E(nc.sbuf_tensor("w8", [1, 512], F32))
        r = lambda i: row_sb[0:1, i * 64:(i + 1) * 64]
        bre = brr_sb[0:1, 0:64]
        bim = brr_sb[0:1, 64:128]
        cmul_conj(r(6), r(7), bre, bim, w8[0:1, 0:64], w8[0:1, 64:128])
        cmul_conj(r(6), r(7), r(2), r(3), w8[0:1, 128:192], w8[0:1, 192:256])
        cmul_conj(r(4), r(5), bre, bim, w8[0:1, 256:320], w8[0:1, 320:384])
        cmul_conj(r(4), r(5), r(2), r(3), w8[0:1, 384:448], w8[0:1, 448:512])
        VW(lambda e: e.nop(), inc="evk")
        evk_w8 = P.mark("evk")
        for i in range(8):
            SC("tensor", lambda e, i=i: e.matmul(
                ps_s[:, 0:64], ones_sb[:], w8[0:1, i * 64:(i + 1) * 64],
                start=True, stop=True),
                waits=[("evk", evk_w8), ("evk", None)], inc="mmk")
            V(lambda e, i=i: e.tensor_copy(wb_sb[:, i, :], ps_s[:, 0:64]),
              waits=[("mmk", None)], inc="evk")
        for i in range(2):
            SC("tensor", lambda e, i=i: e.matmul(
                ps_s[:, 0:64], ones_sb[:], r(i), start=True, stop=True),
                waits=[("evk", None)], inc="mmk")
            V(lambda e, i=i: e.tensor_copy(wb_sb[:, 8 + i, :], ps_s[:, 0:64]),
              waits=[("mmk", None)], inc="evk")
        for jc in range(32):
            V(lambda e, jc=jc: e.tensor_scalar(
                cau[:, 0, :], wb_sb[:, 8, :], cplx[:, 5, jc:jc + 1], None,
                op0=ALU.subtract), waits=[("vch", None)])
            V(lambda e, jc=jc: e.tensor_scalar(
                cau[:, 1, :], wb_sb[:, 9, :], cplx[:, 6, jc:jc + 1], None,
                op0=ALU.subtract))
            VW(lambda e: e.tensor_mul(cau[:, 4, :], cau[:, 0, :], cau[:, 0, :]))
            V(lambda e: e.tensor_mul(cau[:, 5, :], cau[:, 1, :], cau[:, 1, :]))
            VW(lambda e: e.tensor_add(cau[:, 4, :], cau[:, 4, :], cau[:, 5, :]))
            VW(lambda e: e.reciprocal(cau[:, 2, :], cau[:, 4, :]))
            VW(lambda e: e.tensor_scalar_mul(cau[:, 3, :], cau[:, 2, :], -1.0))
            for wi in range(4):
                wre = wb_sb[:, 2 * wi, :]
                wim = wb_sb[:, 2 * wi + 1, :]
                V(lambda e, wre=wre: e.tensor_mul(cau[:, 4, :], wre, cau[:, 0, :]),
                  waits=[("vch", None)])
                V(lambda e, wim=wim: e.tensor_mul(cau[:, 5, :], wim, cau[:, 1, :]))
                V(lambda e, wre=wre: e.tensor_mul(cau[:, 6, :], wre, cau[:, 1, :]))
                V(lambda e, wim=wim: e.tensor_mul(cau[:, 7, :], wim, cau[:, 0, :]))
                VW(lambda e: e.tensor_add(cau[:, 4, :], cau[:, 4, :], cau[:, 5, :]))
                V(lambda e: e.tensor_sub(cau[:, 6, :], cau[:, 6, :], cau[:, 7, :]))
                VW(lambda e: e.tensor_mul(cau[:, 4, :], cau[:, 4, :], cau[:, 3, :]))
                V(lambda e: e.tensor_mul(cau[:, 6, :], cau[:, 6, :], cau[:, 2, :]))
                VW(lambda e, wi=wi, jc=jc: e.reduce_sum(
                    kc_sb[:, 2 * wi, jc:jc + 1], cau[:, 4, :],
                    axis=mybir.AxisListType.X))
                V(lambda e, wi=wi, jc=jc: e.reduce_sum(
                    kc_sb[:, 2 * wi + 1, jc:jc + 1], cau[:, 6, :],
                    axis=mybir.AxisListType.X))
        k = lambda i: kc_sb[:, i, :]
        VW(lambda e: e.tensor_scalar_add(cplx[:, 11, :], k(6), 1.0))
        V(lambda e: e.tensor_mul(cplx[:, 12, :], k(7), k(7)))
        VW(lambda e: e.tensor_mul(cplx[:, 13, :], cplx[:, 11, :], cplx[:, 11, :]))
        VW(lambda e: e.tensor_add(cplx[:, 13, :], cplx[:, 13, :], cplx[:, 12, :]))
        VW(lambda e: e.reciprocal(cplx[:, 13, :], cplx[:, 13, :]))
        V(lambda e: e.tensor_mul(cplx[:, 14, :], k(2), k(4)))
        V(lambda e: e.tensor_mul(cplx[:, 15, :], k(3), k(5)))
        V(lambda e: e.tensor_mul(cplx[:, 16, :], k(2), k(5)))
        V(lambda e: e.tensor_mul(cplx[:, 17, :], k(3), k(4)))
        VW(lambda e: e.tensor_sub(cplx[:, 14, :], cplx[:, 14, :], cplx[:, 15, :]))
        V(lambda e: e.tensor_add(cplx[:, 16, :], cplx[:, 16, :], cplx[:, 17, :]))
        VW(lambda e: e.tensor_mul(cplx[:, 18, :], cplx[:, 14, :], cplx[:, 11, :]))
        V(lambda e: e.tensor_mul(cplx[:, 19, :], cplx[:, 16, :], k(7)))
        V(lambda e: e.tensor_mul(cplx[:, 20, :], cplx[:, 16, :], cplx[:, 11, :]))
        V(lambda e: e.tensor_mul(cplx[:, 21, :], cplx[:, 14, :], k(7)))
        VW(lambda e: e.tensor_add(cplx[:, 18, :], cplx[:, 18, :], cplx[:, 19, :]))
        V(lambda e: e.tensor_sub(cplx[:, 20, :], cplx[:, 20, :], cplx[:, 21, :]))
        VW(lambda e: e.tensor_mul(cplx[:, 18, :], cplx[:, 18, :], cplx[:, 13, :]))
        V(lambda e: e.tensor_mul(cplx[:, 20, :], cplx[:, 20, :], cplx[:, 13, :]))
        VW(lambda e: e.tensor_sub(cplx[:, 18, :], k(0), cplx[:, 18, :]))
        V(lambda e: e.tensor_sub(cplx[:, 20, :], k(1), cplx[:, 20, :]))
        VW(lambda e: e.tensor_mul(cplx[:, 21, :], cplx[:, 7, :], cplx[:, 18, :]))
        V(lambda e: e.tensor_mul(cplx[:, 22, :], cplx[:, 8, :], cplx[:, 20, :]))
        V(lambda e: e.tensor_mul(cplx[:, 23, :], cplx[:, 7, :], cplx[:, 20, :]))
        V(lambda e: e.tensor_mul(cplx[:, 12, :], cplx[:, 8, :], cplx[:, 18, :]))
        VW(lambda e: e.tensor_sub(cplx[:, 21, :], cplx[:, 21, :], cplx[:, 22, :]))
        V(lambda e: e.tensor_add(cplx[:, 23, :], cplx[:, 23, :], cplx[:, 12, :]))
        VW(lambda e: e.nop(), inc="evk")
        evk_ar = P.mark("evk")
        SC("sync", lambda e: e.dma_start(out=ar_re_buf, in_=cplx[:, 21, :]),
           waits=[("evk", evk_ar)], inc="arst", inc_n=16)
        SC("sync", lambda e: e.dma_start(out=ar_im_buf, in_=cplx[:, 23, :]),
           inc="arst", inc_n=16)
        mtre_ap = bass.AP(tensor=ar_re_buf.tensor, offset=0, ap=[[64, 64], [1, 64]])
        mtim_ap = bass.AP(tensor=ar_im_buf.tensor, offset=0, ap=[[64, 64], [1, 64]])
        SC("sync", lambda e: e.dma_start(out=mt_sb[:, 0:64], in_=mtre_ap),
           waits=[("arst", None)], inc="ld", inc_n=16)
        SC("sync", lambda e: e.dma_start(out=mt_sb[:, 64:128], in_=mtim_ap),
           inc="ld", inc_n=16)
        ld_mt = P.mark("ld")
        SC("tensor", lambda e: e.matmul(ps_s[0:64, 0:64], mt_sb[:, 0:64], cst64[:, 0, :],
                                        start=True, stop=False),
           waits=[("ld", ld_mt), ("evk", None)], inc=None)
        SC("tensor", lambda e: e.matmul(ps_s[0:64, 0:64], mt_sb[:, 64:128], cst64[:, 2, :],
                                        start=False, stop=True), inc="mmk")
        SC("tensor", lambda e: e.matmul(ps_s[0:64, 64:128], mt_sb[:, 0:64], cst64[:, 1, :],
                                        start=True, stop=False), inc=None)
        SC("tensor", lambda e: e.matmul(ps_s[0:64, 64:128], mt_sb[:, 64:128], cst64[:, 0, :],
                                        start=False, stop=True), inc="mmk")
        mmk_s1 = P.mark("mmk")
        V(lambda e: e.tensor_mul(cau[0:64, 0, :], ps_s[0:64, 0:64], cst64[:, 5, :]),
          waits=[("mmk", mmk_s1)])
        V(lambda e: e.tensor_mul(cau[0:64, 1, :], ps_s[0:64, 64:128], cst64[:, 6, :]))
        V(lambda e: e.tensor_mul(cau[0:64, 2, :], ps_s[0:64, 0:64], cst64[:, 6, :]))
        V(lambda e: e.tensor_mul(cau[0:64, 3, :], ps_s[0:64, 64:128], cst64[:, 5, :]))
        VW(lambda e: e.tensor_sub(s2_sb[:, 0:64], cau[0:64, 0, :], cau[0:64, 1, :]))
        V(lambda e: e.tensor_add(s2_sb[:, 64:128], cau[0:64, 2, :], cau[0:64, 3, :]))
        VW(lambda e: e.nop(), inc="evk")
        evk_s2 = P.mark("evk")
        SC("tensor", lambda e: e.matmul(ps_s[0:64, 128:192], cst64[:, 3, :],
                                        s2_sb[:, 0:64], start=True, stop=False),
           waits=[("evk", evk_s2)], inc=None)
        SC("tensor", lambda e: e.matmul(ps_s[0:64, 128:192], cst64[:, 4, :],
                                        s2_sb[:, 64:128], start=False, stop=True),
           inc="mmk")
        V(lambda e: e.tensor_copy(kf_sb[:], ps_s[0:64, 128:192]),
          waits=[("mmk", None)], inc="evk")
        evk_kf = P.mark("evk")
        V(lambda e: e.memset(zz_sb[:], 0.0), inc="evk")
        SC("sync", lambda e: e.dma_start(out=kp_buf[0:4096], in_=zz_sb[:]),
           waits=[("evk", None)], inc="kpst", inc_n=16)
        SC("sync", lambda e: e.dma_start(out=kp_buf[4096:8192], in_=zz_sb[:]),
           inc="kpst", inc_n=16)
        kp_ap = bass.AP(tensor=kp_buf.tensor, offset=4095, ap=[[64, 64], [1, 64]])
        SC("sync", lambda e: e.dma_start(out=kp_ap, in_=kf_sb[:]),
           waits=[("evk", evk_kf), ("kpst", None)], inc="kpst", inc_n=16)
        kp_ready = P.mark("kpst")

        SC("tensor", lambda e: e.matmul(ps_sc[:, 4:8], ones_sb[:], gbrow_sb[:],
                                        start=True, stop=True),
           waits=[("ld", ld_all), ("evk", None)], inc="mmk")
        V(lambda e: e.tensor_copy(gb_sb[:], ps_sc[:, 4:8]), waits=[("mmk", None)],
          inc="evk")
        for half in range(2):
            sl = slice(half * 512, half * 512 + 512)
            for dc in range(32):
                SC("tensor", lambda e, dc=dc, sl=sl: e.matmul(
                    ps_pt[:, sl], at_sb[:, dc, :], xnt[:, dc, sl],
                    start=(dc == 0), stop=(dc == 31)),
                    waits=([("ld", ld_all)] if dc == 0 else []),
                    inc=("mmp" if (half == 1 and dc == 31) else None))
        for tc in range(8):
            for dc in range(32):
                SC("tensor", lambda e, dc=dc, tc=tc: e.matmul(
                    ps_sc[:, 0:4], xnt[:, dc, tc * 128:(tc + 1) * 128], gw_sb[:, dc, :],
                    start=(dc == 0), stop=(dc == 31)),
                    waits=([("scm", None), ("ld", ld_all)] if dc == 0 else []),
                    inc=("mmk" if dc == 31 else None))
            V(lambda e: e.tensor_add(sc_sb[:, 0:4], ps_sc[:, 0:4], gb_sb[:]),
              waits=[("mmk", None), ("evk", None)])
            VW(lambda e: e.reduce_max(sc_sb[:, 8:9], sc_sb[:, 0:4],
                                      axis=mybir.AxisListType.X))
            VW(lambda e: e.tensor_scalar_mul(sc_sb[:, 16:17], sc_sb[:, 8:9], -1.0))
            VW(lambda e: e.nop(), inc="scm")
            SC("scalar", lambda e: e.activation(sc_sb[:, 4:8], sc_sb[:, 0:4], AF.Exp,
                                                bias=sc_sb[:, 16:17]),
               waits=[("scm", None)], inc="sch")
            V(lambda e: e.reduce_sum(sc_sb[:, 9:10], sc_sb[:, 4:8],
                                     axis=mybir.AxisListType.X),
              waits=[("sch", None)])
            VW(lambda e: e.reciprocal(sc_sb[:, 10:11], sc_sb[:, 9:10]))
            VW(lambda e: e.tensor_scalar(sc_sb[:, 4:8], sc_sb[:, 4:8], sc_sb[:, 10:11],
                                         None, op0=ALU.mult))
            VW(lambda e: e.reduce_max(sc_sb[:, 17:18], sc_sb[:, 4:8],
                                      axis=mybir.AxisListType.X))
            VW(lambda e: e.memset(sc_sb[:, 11:12], 0.0))
            for rr in range(4):
                VW(lambda e, rr=rr: e.tensor_scalar(
                    sc_sb[:, 12 + rr:13 + rr], sc_sb[:, 4 + rr:5 + rr], sc_sb[:, 17:18],
                    None, op0=ALU.is_ge))
                VW(lambda e, rr=rr: e.tensor_scalar(
                    sc_sb[:, 18:19], sc_sb[:, 11:12], -1.0, 1.0,
                    op0=ALU.mult, op1=ALU.add))
                VW(lambda e, rr=rr: e.tensor_mul(
                    sc_sb[:, 12 + rr:13 + rr], sc_sb[:, 12 + rr:13 + rr], sc_sb[:, 18:19]))
                VW(lambda e, rr=rr: e.tensor_add(
                    sc_sb[:, 11:12], sc_sb[:, 11:12], sc_sb[:, 12 + rr:13 + rr]))
            VW(lambda e: e.tensor_scalar_mul(rs_sb[:], sc_sb[:, 12:13], 8.0))
            for rr, wgt in ((1, 16.0), (2, 32.0), (3, 64.0)):
                VW(lambda e, rr=rr, wgt=wgt: e.tensor_scalar_mul(
                    sc_sb[:, 19:20], sc_sb[:, 12 + rr:13 + rr], wgt))
                VW(lambda e: e.tensor_add(rs_sb[:], rs_sb[:], sc_sb[:, 19:20]))
            VW(lambda e: e.nop(), inc="vch")
            SC("sync", lambda e, tc=tc: e.dma_start(
                out=rs_buf[0:1, tc * 128:(tc + 1) * 128], in_=rs_sb[:]),
                waits=[("vch", None)], inc="rsst", inc_n=16)
        SC("sync", lambda e: e.dma_start(out=rsrow_sb[:], in_=rs_buf),
           waits=[("rsst", None)], inc="ld", inc_n=16)
        ld_rs = P.mark("ld")
        for half in range(2):
            sl = slice(half * 512, half * 512 + 512)
            SC("tensor", lambda e, sl=sl: e.matmul(
                ps_s[0:64, 0:512], ones_sb[0:1, 0:64], rsrow_sb[0:1, sl],
                start=True, stop=True),
                waits=[("ld", ld_rs), ("evk", None)], inc="mmk")
            V(lambda e, sl=sl: e.tensor_scalar(
                msk_sb[:, sl], ps_s[0:64, 0:512], iota_sb[:], None, op0=ALU.is_gt),
              waits=[("mmk", None)], inc="evk")
        SC("vector", lambda e: e.tensor_mul(pm_sb[:], ps_pt[:], msk_sb[:]),
           waits=[("vch", None), ("mmp", 1)], inc="evk")
        evk_pm = P.mark("evk")

        for oc in range(8):
            njc = 4 * oc + 4
            for dc in range(32):
                SC("gpsimd", lambda e, oc=oc, dc=dc: e.dma_start(
                    out=wt_sb[:, dc, :],
                    in_=wt_in[dc * 128:(dc + 1) * 128, oc * 512:(oc + 1) * 512]),
                    waits=([("mmm", None)] if dc == 0 else []), inc="wld", inc_n=16)
            for jc in range(njc):
                off = 3968 - 128 * jc + 512 * oc
                SC("gpsimd", lambda e, off=off, jc=jc: e.dma_start(
                    out=kp_sb[:, jc, :],
                    in_=bass.AP(tensor=kp_buf.tensor, offset=off, ap=[[1, 128], [1, 512]])),
                    waits=([("kpst", kp_ready)] if jc == 0 else []), inc="wld", inc_n=16)
            wld_oc = P.mark("wld")
            SC("tensor", lambda e, oc=oc: e.matmul(
                ps_s[:, 0:512], ones_sb[:], brow_sb[0:1, oc * 512:(oc + 1) * 512],
                start=True, stop=True),
                waits=[("evk", None), ("mmm", None)], inc="mmk")
            V(lambda e: e.tensor_copy(bb_sb[:], ps_s[:, 0:512]),
              waits=[("mmk", None)], inc="evk")
            evk_bb = P.mark("evk")
            for tch in range(2):
                for jc in range(njc):
                    tp = jc % 2
                    SC("sync", lambda e, jc=jc, tp=tp: e.dma_start(
                        out=trev[:, tp, :], in_=xntr_in[jc * 128:(jc + 1) * 128, :]),
                        waits=[("mmm", max(0, P.cnt["mmm"] - 1))], inc="tld", inc_n=16)
                    tld_j = P.mark("tld")
                    for tc4 in range(4):
                        tc = tch * 4 + tc4
                        SC("tensor", lambda e, jc=jc, tp=tp, tc=tc, tc4=tc4, oc=oc: e.matmul(
                            ps4[tc4][:], trev[:, tp, tc * 128:(tc + 1) * 128],
                            kp_sb[:, jc, :], start=(jc == 0), stop=False),
                            waits=([("tld", tld_j), ("wld", wld_oc),
                                    ("ev", None)] if tc4 == 0 else []),
                            inc=("mmm" if tc4 == 3 else None))
                for dc in range(32):
                    for tc4 in range(4):
                        tc = tch * 4 + tc4
                        SC("tensor", lambda e, dc=dc, tc=tc, tc4=tc4: e.matmul(
                            ps4[tc4][:], xnt[:, dc, tc * 128:(tc + 1) * 128],
                            wt_sb[:, dc, :], start=False, stop=False))
                for tc4 in range(4):
                    tc = tch * 4 + tc4
                    SC("tensor", lambda e, tc=tc, tc4=tc4, oc=oc: e.matmul(
                        ps4[tc4][:], pm_sb[:, tc * 128:(tc + 1) * 128],
                        bt_sb[:, oc * 512:(oc + 1) * 512], start=False, stop=True),
                        waits=([("evk", evk_pm)] if tc4 == 0 else []),
                        inc="mmm")
                for tc4 in range(4):
                    tc = tch * 4 + tc4
                    pp = tc4 % 2
                    SC("sync", lambda e, tc=tc, oc=oc, pp=pp: e.dma_start(
                        out=xt_sb[:, pp, :],
                        in_=x1_in[tc * 128:(tc + 1) * 128, oc * 512:(oc + 1) * 512]),
                        waits=[("ev", max(0, P.cnt["ev"] - 1))], inc="xld", inc_n=16)
                    SC("sync", lambda e, tc=tc, oc=oc, pp=pp: e.dma_start(
                        out=ut_sb[:, pp, :],
                        in_=xn1_in[tc * 128:(tc + 1) * 128, oc * 512:(oc + 1) * 512]),
                        inc="xld", inc_n=16)
                    xld_d = P.mark("xld")
                    SC("vector", lambda e, tc4=tc4, pp=pp: e.tensor_add(
                        tmp_sb[:, pp, :], ps4[tc4][:], bb_sb[:]),
                        waits=[("mmm", None), ("evk", evk_bb), ("xld", xld_d),
                               ("sto", max(0, (P.cnt["ev"] - 1) * 16))], inc="vch")
                    SC("vector", lambda e, pp=pp: e.tensor_add(
                        tmp_sb[:, pp, :], tmp_sb[:, pp, :], xt_sb[:, pp, :]),
                        waits=[("vch", None)], inc="vch")
                    SC("vector", lambda e, pp=pp: e.tensor_scalar(
                        x2_sb[:, pp, :], ut_sb[:, pp, :], db_sb[:, 0:1], None,
                        op0=ALU.mult), inc="vch")
                    SC("vector", lambda e, pp=pp: e.tensor_add(
                        x2_sb[:, pp, :], x2_sb[:, pp, :], tmp_sb[:, pp, :]),
                        waits=[("vch", None)], inc="ev")
                    SC("sync", lambda e, tc=tc, oc=oc, pp=pp: e.dma_start(
                        out=x2_out[tc * 128:(tc + 1) * 128, oc * 512:(oc + 1) * 512],
                        in_=x2_sb[:, pp, :]),
                        waits=[("ev", None)], inc="sto", inc_n=16)

        SC("sync", lambda e: None, waits=[("sto", None)])

        @block.sync
        def _(sync):
            P.emit("sync", sync)

        @block.gpsimd
        def _(gpsimd):
            P.emit("gpsimd", gpsimd)

        @block.tensor
        def _(tensor):
            P.emit("tensor", tensor)

        @block.vector
        def _(vector):
            P.emit("vector", vector)

        @block.scalar
        def _(scalar):
            P.emit("scalar", scalar)

    return nc


def l1_host_consts():
    j = np.arange(64)
    w64 = np.exp(2j * np.pi * np.outer(j, j) / 64.0)
    tw = np.exp(2j * np.pi * np.outer(j, j) / 4096.0)
    return dict(
        w64_re=np.ascontiguousarray(w64.real, dtype=np.float32),
        w64_im=np.ascontiguousarray(w64.imag, dtype=np.float32),
        w64_imn=np.ascontiguousarray(-w64.imag, dtype=np.float32),
        wf_re=np.ascontiguousarray(w64.real / 4096.0, dtype=np.float32),
        wf_imn=np.ascontiguousarray(-w64.imag / 4096.0, dtype=np.float32),
        tw_re=np.ascontiguousarray(tw.real, dtype=np.float32),
        tw_im=np.ascontiguousarray(tw.imag, dtype=np.float32),
        ones_row=np.ones((1, 128), np.float32),
        iota64=np.arange(64, dtype=np.float32).reshape(64, 1),
    )


def run_l1(inputs, wts, x1_shards, xn1_shards):
    om = np.exp(-2j * np.pi * (np.arange(4096, dtype=np.float64) / 4096.0)).astype(np.complex64)
    lam = inputs["L1_attn_Lambda_ri"]
    pp = inputs["L1_attn_p_ri"]
    qq = inputs["L1_attn_q_ri"]
    ct = inputs["L1_attn_Ct_ri"].reshape(64, 2)
    rows8 = np.concatenate([lam[:, 0], lam[:, 1], pp[:, 0], pp[:, 1],
                            qq[:, 0], qq[:, 1], ct[:, 0], ct[:, 1]]).reshape(1, 512)
    consts = l1_host_consts()
    in_maps = []
    for c in range(NCORES):
        xnt = np.ascontiguousarray(np.asarray(xn1_shards[c]).T)
        xntr = np.ascontiguousarray(xnt.reshape(32, 128, 1024)[:, ::-1, :].reshape(4096, 1024))
        m = dict(
            x1=np.ascontiguousarray(x1_shards[c]), xn1=np.asarray(xn1_shards[c]),
            xnt=xnt, xntr=xntr,
            l1wt_bf=wts["l1wt"], l1at_bf=wts["l1at"], l1bt_bf=wts["l1bt"],
            l1gwt_bf=wts["l1gwt"],
            b_row=inputs["L1_ffn_b"].reshape(1, -1).astype(np.float32),
            gb_row=inputs["L1_ffn_gate_b"].reshape(1, 4).astype(np.float32),
            d_val=inputs["L1_attn_D"].reshape(1, 1).astype(np.float32),
            log_step=inputs["L1_attn_log_step"].reshape(1, 1).astype(np.float32),
            om_re=np.ascontiguousarray(om.real.reshape(128, 32)),
            om_im=np.ascontiguousarray(om.imag.reshape(128, 32)),
            rows8=np.ascontiguousarray(rows8.astype(np.float32)),
            vct_re=np.ascontiguousarray(inputs["L1_attn_Vc_ri"][:, :, 0].T.astype(np.float32)),
            vct_im=np.ascontiguousarray(inputs["L1_attn_Vc_ri"][:, :, 1].T.astype(np.float32)),
            bp=inputs["L1_attn_Bp"].astype(np.float32),
            **consts,
        )
        in_maps.append(m)
    res = run_bass_kernel_spmd(build_l1(), in_maps, core_ids=list(range(NCORES)))
    return [res.results[c]["x2"] for c in range(NCORES)]



def kernel(**inputs):
    inputs = {k: np.asarray(v) for k, v in inputs.items()}
    wts = run_weights(inputs)
    prep = run_prep(inputs, wts)
    att = run_attn(inputs, prep)
    x1 = run_ffn(inputs, wts, prep, att)
    xn1 = run_ln(x1)
    x2 = run_l1(inputs, wts, x1, xn1)
    out = np.zeros((B * S, D), np.float32)
    for c in range(NCORES):
        out[rows_of_core(c)] = x2[c]
    return out.reshape(B, S, D)
